# version 06
# baseline (speedup 1.0000x reference)
import contextlib

ENGINES = ("tensor", "vector", "scalar", "gpsimd", "sync")

class Prog:
    def __init__(self, nc, sem_names):
        self.nc = nc
        self.ops = {e: [] for e in ENGINES}
        self.sem_names = list(sem_names)
        self.counts = {s: 0 for s in sem_names}
        self.n_ops = 0

    def op(self, eng, fn, waits=(), inc=None, ninc=1):
        waits = tuple((s, v) for w in waits if w is not None for (s, v) in [w] if v > 0)
        for s, v in waits:
            assert v <= self.counts[s], f"wait on future token {s}={v} (at {self.counts[s]})"
        tok = None
        if inc is not None:
            self.counts[inc] += ninc
            tok = (inc, self.counts[inc])
        self.ops[eng].append((fn, waits, inc, ninc))
        self.n_ops += 1
        return tok

    def dma(self, eng, out, in_, waits=(), q=None, **kw):
        sem = q or ("q" + eng)
        waits = list(waits) + [(sem, self.counts[sem])]
        return self.op(eng, lambda e: e.dma_start(out=out, in_=in_, **kw),
                       waits=waits, inc=sem, ninc=16)

    def tok(self, sem):
        return (sem, self.counts[sem])

    def build(self):
        nc = self.nc
        with contextlib.ExitStack() as st:
            st.enter_context(nc.allow_low_precision(
                "intentional bf16 outputs (probs/normalizers); accumulations stay fp32"))
            sems = {s: st.enter_context(nc.semaphore(s)) for s in self.sem_names}
            block = st.enter_context(nc.Block())
            for eng in ENGINES:
                if not self.ops[eng]:
                    continue
                def body(e, _eng=eng):
                    last_wait = {}
                    for fn, waits, inc, ninc in self.ops[_eng]:
                        for s, v in waits:
                            if last_wait.get(s, -1) < v:
                                e.wait_ge(sems[s], v)
                                last_wait[s] = v
                        ins = fn(e)
                        if inc is not None:
                            assert ins is not None
                            ins.then_inc(sems[inc], ninc)
                getattr(block, eng)(body)
        return nc


"""Megakernel builder v2: arena memory plan + phase fences. One core = one batch row."""
import numpy as np
import concourse.bass as bass
import concourse.mybir as mybir
from concourse.alu_op_type import AluOpType


F32 = mybir.dt.float32; BF16 = mybir.dt.bfloat16; I32 = mybir.dt.int32
U8 = mybir.dt.uint8
AF = mybir.ActivationFunctionType
AX = mybir.AxisListType
E50 = float(np.exp(np.float32(-50.0)))
EP50 = float(np.exp(np.float32(50.0)))

def real_cfg():
    return dict(S=2048, D=2048, F0=8192, F1=8192, NH0=16, KV0=4, HD0=128,
                NH1=32, KV1=4, HD1=64, TT=512)

def dev_cfg():
    return dict(S=512, D=256, F0=512, F1=512, NH0=2, KV0=1, HD0=128,
                NH1=4, KV1=2, HD1=64, TT=256)

SEMS = ["qx", "qst", "qsb", "qscw", "qscx", "swA", "swB", "swC", "swD",
        "gwA", "gwB", "gwC", "gwD", "qgp", "T", "V", "S", "G"]

class MK:
    SBUF_TOP = 229344
    ARENA_KB = 191.5

    def __init__(self, cfg):
        self.cfg = dict(cfg)
        c = self.cfg
        assert c["HD0"] == 128
        c["C"] = c["D"] // 128
        c["NT"] = c["S"] // c["TT"]
        c["KPG"] = c["TT"] // 128
        c["CF0"] = c["F0"] // 128
        c["CF1"] = c["F1"] // 128
        c["TC"] = c["S"] // 128
        self.nc = bass.Bass("TRN2", target_bir_lowering=False)
        self.P = Prog(self.nc, SEMS)
        self.sb = {}
        self.ps = {}
        self.dram = {}
        self.slotdep = {}
        self.bankdep = {}
        self._guards = {}
        self._asz = int(self.ARENA_KB * 1024)
        self._aoff = self.SBUF_TOP - self._asz
        self._frl = [(0, self._asz)]
        self._live = {}

    def sbuf(self, name, shape, dt, small=False):
        nbytes = int(np.prod(shape[1:])) * {F32: 4, BF16: 2, I32: 4, U8: 1}[dt]
        nbytes = (nbytes + 31) // 32 * 32
        if small:
            g = self.nc.sbuf_tensor(name, list(shape), dt)
            ap = g.__enter__().ap()
            self.sb[name] = ap
            self._guards[name] = g
            return ap
        cands = [(sz, st, i) for i, (st, sz) in enumerate(self._frl) if sz >= nbytes]
        if cands:
            sz, st, i = min(cands)
            if sz == nbytes:
                self._frl.pop(i)
            else:
                self._frl[i] = (st + nbytes, sz - nbytes)
            self._live[name] = (st, nbytes)
            used = sum(v[1] for v in self._live.values())
            if used > getattr(self, "_apeak", 0):
                self._apeak = used
                self._apeak_live = dict(self._live)
            ap = self.nc.alloc_sbuf_tensor_at(name, list(shape), dt,
                                              offset=self._aoff + st).ap()
            self.sb[name] = ap
            return ap
        raise RuntimeError(
            f"arena full: {name} needs {nbytes}B; live="
            f"{sorted(((v[0], v[1], k) for k, v in self._live.items()))} free={self._frl}")

    def free(self, *names):
        for name in names:
            assert name not in self._guards, f"cannot free native-allocated {name}"
            st, nb = self._live.pop(name)
            self.sb.pop(name, None)
            self._frl.append((st, nb))
            self._frl.sort()
            out = []
            for s, z in self._frl:
                if out and out[-1][0] + out[-1][1] == s:
                    out[-1] = (out[-1][0], out[-1][1] + z)
                else:
                    out.append((s, z))
            self._frl = out

    def arena_check(self):
        rem = self.nc.sbuf_bytes_remaining
        assert rem >= self._asz, f"normal allocator reached arena: remaining={rem}"

    def psum(self, name, shape, dt=F32):
        ap = self.nc.alloc_psum_tensor(name, list(shape), dt).ap()
        self.ps[name] = ap
        return ap

    def din(self, name, shape, dt):
        ap = self.nc.dram_tensor(name, list(shape), dt, kind="ExternalInput").ap()
        self.dram[name] = ap
        return ap

    def dout(self, name, shape, dt):
        ap = self.nc.dram_tensor(name, list(shape), dt, kind="ExternalOutput").ap()
        self.dram[name] = ap
        return ap

    def dscr(self, name, shape, dt):
        ap = self.nc.dram_tensor(name, list(shape), dt, kind="Internal").ap()
        self.dram[name] = ap
        return ap

    def mm(self, out, lhsT, rhs, start, stop, waits=(), force_inc=False):
        def f(e, o=out, l=lhsT, r=rhs, st=start, sp=stop):
            return e.matmul(o, l, r, start=st, stop=sp)
        return self.P.op("tensor", f, waits=waits, inc="T" if (stop or force_inc) else None)

    def vop(self, fn, waits=(), inc="V"):
        return self.P.op("vector", fn, waits=waits, inc=inc)

    def sop(self, fn, waits=(), inc="S"):
        return self.P.op("scalar", fn, waits=waits, inc=inc)

    def gop(self, fn, waits=(), inc="G"):
        return self.P.op("gpsimd", fn, waits=waits, inc=inc)

    def bw(self, name):
        t = self.bankdep.get(name)
        return [t] if t else []

    def bs(self, name, tok):
        self.bankdep[name] = tok

    def fence(self):
        toks = [self.P.tok(s) for s in SEMS]
        incs = {"tensor": "T", "vector": "V", "scalar": "S", "gpsimd": "G", "sync": None}
        for eng in ENGINES:
            self.P.op(eng, lambda e: e.nop(), waits=toks, inc=incs[eng])

    def seed_slot(self, dst, srckey):
        t = self.slotdep.get(srckey)
        if t is not None and self.slotdep.get(dst) is None:
            self.slotdep[dst] = t

    def setup(self):
        c = self.cfg
        S, TT, C, KPG = c["S"], c["TT"], c["C"], c["KPG"]
        sb = self.sbuf
        self.psum("p0", [128, 512]); self.psum("p1", [128, 512])
        self.psum("pc", [128, 512]); self.psum("ub", [128, 512])
        self.psum("u2", [128, 512])
        self.psum("u0", [1, 512]); self.psum("u1", [1, 512]); self.psum("pd", [1, 512])
        sb("ones_col", [128, 1], BF16, small=True)
        sb("ones_colf", [128, 1], F32, small=True)
        sb("ones_row", [1, 128], BF16, small=True)
        sb("ones_rowf", [1, 128], F32, small=True)
        sb("ones_rowT", [1, TT], BF16, small=True)
        sb("zrow", [1, 128], BF16, small=True)
        sb("eps_col", [128, 1], F32, small=True)
        sb("c88_col", [128, 1], F32, small=True)
        sb("identf", [128, 128], F32, small=True)
        sb("row_bf", [1, 512], BF16, small=True)
        sb("row_bf2", [1, 512], BF16, small=True)
        sb("mu_t", [128, 512], BF16, small=True)
        sb("r_t", [128, 512], BF16, small=True)
        sb("rq_t", [128, 512], BF16, small=True)
        sb("cch", [128, 512], BF16, small=True)
        sb("pr0", [128, 512], BF16, small=True)
        sb("pr1", [128, 512], BF16, small=True)
        sb("csrow", [1, 128], BF16, small=True)
        sb("rcolk", [128, KPG], F32, small=True)
        sb("mcolq", [128, 1], F32, small=True)
        g1 = self.gop(lambda e: e.memset(self.sb["ones_col"], 1.0))
        self.gop(lambda e: e.memset(self.sb["ones_colf"], 1.0))
        self.gop(lambda e: e.memset(self.sb["ones_row"], 1.0))
        self.gop(lambda e: e.memset(self.sb["ones_rowf"], 1.0))
        self.gop(lambda e: e.memset(self.sb["ones_rowT"], 1.0))
        self.gop(lambda e: e.memset(self.sb["zrow"], 0.0))
        self.gop(lambda e: e.memset(self.sb["eps_col"], 1e-5))
        self.gop(lambda e: e.memset(self.sb["c88_col"], 88.0))
        it = self.sbuf("iota_i", [128, 512], I32)
        itf = self.sbuf("iota_f", [128, 512], F32)
        self.gop(lambda e: e.iota(it[:, 0:TT], [[1, TT]], channel_multiplier=-1))
        t_g = self.P.tok("G")
        t_f = self.vop(lambda e: e.tensor_copy(itf[:, 0:TT], it[:, 0:TT]), waits=[t_g])
        t_id = self.vop(lambda e: e.tensor_single_scalar(self.sb["identf"], itf[:, 0:128],
                        0.0, AluOpType.is_equal), waits=[t_f])
        self.t_ident = [t_id]
        W = TT + (KPG - 1) * 128
        masks = self.sbuf("masks0", [128, W], BF16, small=True)
        itw = self.sbuf("iota_w", [128, W], I32)
        itwf = self.sbuf("iota_wf", [128, W], F32)
        self.gop(lambda e: e.iota(itw, [[1, W]], channel_multiplier=-1))
        t_gw2 = self.P.tok("G")
        t_fw = self.vop(lambda e: e.tensor_copy(itwf, itw), waits=[t_gw2])
        t_m = self.vop(lambda e: e.tensor_single_scalar(
            masks, itwf, float(128 * (KPG - 1)), AluOpType.is_lt), waits=[t_fw])
        t_m2 = self.vop(lambda e: e.tensor_scalar_mul(masks, masks, -1e9), waits=[t_m])
        self.t_masks = [t_m2]
        self._mask_slice = lambda r: masks[:, (KPG - 1 - r) * 128:(KPG - 1 - r) * 128 + TT]
        mdq = self.sbuf("mdiagq", [128, 128], BF16, small=True)
        t_md0 = self.vop(lambda e: e.tensor_single_scalar(mdq, itf[:, 0:128],
                         0.0, AluOpType.is_gt), waits=[t_f])
        self.t_mdq = self.vop(lambda e: e.tensor_scalar_mul(mdq, mdq, -1e9), waits=[t_md0])
        self.fence()
        self.free("iota_i", "iota_f", "iota_w", "iota_wf")
        self.sbuf("gw0", [128, c["C"] * 128], BF16)
        self.RGA = self.sbuf("RGA", [128, c["C"] * TT], BF16)
        self._sv_id = 0
        self.t_setup = self.P.tok("G")

    def load_x(self):
        c = self.cfg
        X = self.sbuf("X", [128, c["C"] * c["S"]], F32)
        xin = self.din("x_in", [128, c["C"] * c["S"]], F32)
        n = c["C"] * c["S"]
        q = n // 4
        for i in range(4):
            self.P.dma("sync", X[:, i * q:(i + 1) * q], xin[:, i * q:(i + 1) * q], q="qx")
        self.t_x = self.P.tok("qx")

    def stats_fold(self, xsrc, n, nchunks, waits, want_rq=False, want_mu_b=False,
                   inv_dim=None):
        P = self.P
        ones_c = self.sb["ones_col"]
        u0, u1, pd = self.ps["u0"], self.ps["u1"], self.ps["pd"]
        cch, sq = self.sb["cch"], self.sb["pr1"]
        inv_dim = inv_dim if inv_dim is not None else 1.0 / (nchunks * 128)
        t_prev_sq = None
        t_prev_c = None
        for cc in range(nchunks):
            t_c = self.vop(lambda e, cc=cc: e.tensor_copy(cch[:, 0:n], xsrc(cc)),
                           waits=list(waits) + ([t_prev_sq] if t_prev_sq else []))
            t_s = self.vop(lambda e: e.tensor_mul(sq[:, 0:n], cch[:, 0:n], cch[:, 0:n]),
                           waits=[t_c] + ([self.P.tok("S"), self.P.tok("T")] if cc == 0 else []))
            self.mm(u0[0:1, 0:n], ones_c, cch[:, 0:n], start=(cc == 0),
                    stop=(cc == nchunks - 1),
                    waits=[t_c] + (self.bw("u0") if cc == 0 else []), force_inc=True)
            t_prev_c = self.P.tok("T")
            t_mm2 = self.mm(u1[0:1, 0:n], ones_c, sq[:, 0:n], start=(cc == 0),
                            stop=(cc == nchunks - 1),
                            waits=[t_s] + (self.bw("u1") if cc == 0 else []), force_inc=True)
            t_prev_sq = t_mm2
        t_mm = self.P.tok("T")
        t7 = self.vop(lambda e: e.tensor_scalar_mul(self.sb["row_bf"][0:1, 0:n],
                      u0[0:1, 0:n], -inv_dim), waits=[t_mm])
        self.bs("u0", t7)
        t2 = self.vop(lambda e: e.tensor_scalar_mul(u1[0:1, 0:n], u1[0:1, 0:n], inv_dim),
                      waits=[t_mm])
        t3 = self.vop(lambda e: e.tensor_mul(self.sb["row_bf2"][0:1, 0:n],
                      self.sb["row_bf"][0:1, 0:n], self.sb["row_bf"][0:1, 0:n]),
                      waits=[t7])
        t4 = self.vop(lambda e: e.tensor_sub(u1[0:1, 0:n], u1[0:1, 0:n],
                      self.sb["row_bf2"][0:1, 0:n]), waits=[t2, t3])
        t5 = self.sop(lambda e: e.activation(u1[0:1, 0:n], u1[0:1, 0:n], AF.Sqrt,
                      bias=self.sb["eps_col"][0:1, :]), waits=[t4, self.t_setup])
        t6 = self.vop(lambda e: e.reciprocal(pd[0:1, 0:n], u1[0:1, 0:n]),
                      waits=[t5] + self.bw("pd"))
        t8 = self.vop(lambda e: e.tensor_copy(self.sb["row_bf2"][0:1, 0:n], pd[0:1, 0:n]),
                      waits=[t6])
        self.bs("pd", t8)
        self.bs("u1", t8)
        out = {"nmu_row": t7, "r_row": t8}
        tb = self.mm(self.ps["u2"][:, 0:n], self.sb["ones_row"], self.sb["row_bf2"][0:1, 0:n],
                     start=True, stop=True, waits=[t8] + self.bw("u2"))
        te = self.vop(lambda e: e.tensor_copy(self.sb["r_t"][:, 0:n], self.ps["u2"][:, 0:n]),
                      waits=[tb])
        out["r_b"] = te
        if want_rq:
            SQ128 = float(np.sqrt(np.float32(128.0)))
            te2 = self.vop(lambda e: e.tensor_scalar_mul(self.sb["rq_t"][:, 0:n],
                           self.ps["u2"][:, 0:n], SQ128), waits=[tb])
            out["rq_b"] = te2
            te = te2
        if want_mu_b:
            tb2 = self.mm(self.ps["u2"][:, 0:n], self.sb["ones_row"],
                          self.sb["row_bf"][0:1, 0:n], start=True, stop=True,
                          waits=[te, t7])
            tm = self.vop(lambda e: e.tensor_copy(self.sb["mu_t"][:, 0:n],
                          self.ps["u2"][:, 0:n]), waits=[tb2])
            out["nmu_b"] = tm
            self.bs("u2", tm)
        else:
            self.bs("u2", te)
        return out

    def proj_fold(self, wsrc, n_oc, n_dc, xb, n, evac, waits, cast, nmu_tok, wkey,
                  nslots=2):
        weng = "gpsimd" if cast else "sync"
        sems = ["gwA", "gwB"] if cast else ["swC", "swD"]
        wbufs = [self.sb[wkey + "0"]] + ([self.sb[wkey + "1"]] if nslots > 1 else [])
        toks = []
        for oc in range(n_oc):
            slot = oc % nslots
            wb = wbufs[slot]
            dep = self.slotdep.get(wkey + str(slot))
            t_w = self.P.dma(weng, wb[:, 0:n_dc * 128], wsrc(oc),
                             waits=([dep] if dep else []), q=sems[slot])
            cs = self.sb["csrow"]
            t_cs_mm = None
            for dc in range(n_dc):
                t_cs_mm = self.mm(self.ps["u0"][0:1, 0:128], self.sb["ones_col"],
                                  wb[:, dc * 128:(dc + 1) * 128], start=(dc == 0),
                                  stop=(dc == n_dc - 1),
                                  waits=[t_w] + (self.bw("u0") if dc == 0 else []))
            t_cs = self.vop(lambda e, cs=cs: e.tensor_copy(cs[0:1, :], self.ps["u0"][0:1, 0:128]),
                            waits=[t_cs_mm, self.bankdep.get("csrow")])
            self.bs("u0", t_cs)
            pb = ["p0", "p1"][oc % 2]
            psm = self.ps[pb]
            t_mm = None
            for dc in range(n_dc):
                t_mm = self.mm(psm[:, 0:n], wb[:, dc * 128:(dc + 1) * 128],
                               xb[:, dc * n:(dc + 1) * n],
                               start=(dc == 0), stop=False,
                               waits=[t_w] + list(waits) + (self.bw(pb) if dc == 0 else []))
            t_mm = self.mm(psm[:, 0:n], cs[0:1, :], self.sb["row_bf"][0:1, 0:n],
                           start=False, stop=True, waits=[t_cs, nmu_tok])
            self.bs("csrow", t_mm)
            self.slotdep[wkey + str(slot)] = t_mm
            t_ev = evac(oc, psm, t_mm)
            self.bs(pb, t_ev)
            toks.append(t_ev)
        return toks

    def l0_phaseA(self):
        c = self.cfg
        S, TT, C, KV0, NT, KPG, CF0 = c["S"], c["TT"], c["C"], c["KV0"], c["NT"], c["KPG"], c["CF0"]
        V0 = KV0 * 128
        X = self.sb["X"]
        XB = self.RGA
        self.sb["XB"] = XB
        k0 = self.sbuf("k0", [128, KV0 * S], BF16)
        v0t = self.sbuf("v0t", [128, (S // 128) * V0], BF16)
        self.sbuf("vw0", [128, V0], BF16)
        self.sbuf("vw1", [128, V0], BF16)
        wk = self.din("wk_prep", [KV0, 128, C * 128], F32)
        wv = self.din("wv_prep", [C, 128, V0], F32)
        vw = [self.sb["vw0"], self.sb["vw1"]]
        self.sbuf("vcs", [1, V0], BF16, small=True)
        t_cs = None
        for dc in range(C):
            slot = dc % 2
            dep = self.slotdep.get("vw" + str(slot))
            t_w = self.P.dma("gpsimd", vw[slot][:, 0:V0], wv[dc],
                             waits=([dep] if dep else []), q=["gwC", "gwD"][slot])
            t_cs = self.mm(self.ps["u0"][0:1, 0:V0], self.sb["ones_col"], vw[slot][:, 0:V0],
                           start=(dc == 0), stop=(dc == C - 1),
                           waits=[t_w] + (self.bw("u0") if dc == 0 else []), force_inc=True)
            self.slotdep["vw" + str(slot)] = t_cs
        self._t_vcs = self.vop(lambda e: e.tensor_copy(self.sb["vcs"][0:1, :],
                               self.ps["u0"][0:1, 0:V0]), waits=[t_cs])
        self.bs("u0", self._t_vcs)
        for g in range(NT):
            st = self.stats_fold(
                lambda cc, g=g: X[:, cc * S + g * TT: cc * S + (g + 1) * TT],
                TT, C, waits=[self.t_x, self.P.tok("T")])
            t_xb = []
            tprev = [self.P.tok("T"), self.t_x]
            for cc in range(C):
                t_xb.append(self.vop(lambda e, cc=cc, g=g: e.tensor_copy(
                    XB[:, cc * TT:(cc + 1) * TT],
                    X[:, cc * S + g * TT: cc * S + (g + 1) * TT]), waits=tprev))
            t_rc = []
            for tc in range(KPG):
                t_t = self.mm(self.ps["u2"][:, 0:1],
                              self.sb["row_bf2"][0:1, tc * 128:(tc + 1) * 128],
                              self.sb["ones_col"][0:1, 0:1], start=True, stop=True,
                              waits=[st["r_row"]] + self.bw("u2"))
                t_e = self.vop(lambda e, tc=tc: e.tensor_copy(
                    self.sb["rcolk"][:, tc:tc + 1], self.ps["u2"][:, 0:1]), waits=[t_t])
                self.bs("u2", t_e)
                t_rc.append(t_e)
            def k_evac(oc, psm, t_mm, g=g, st=st):
                dst = k0[:, oc * S + g * TT: oc * S + (g + 1) * TT]
                return self.vop(lambda e, d=dst, p=psm: e.tensor_mul(
                    d, p[:, 0:TT], self.sb["r_t"][:, 0:TT]), waits=[t_mm, st["r_b"]])
            self.proj_fold(lambda oc: wk[oc], KV0, C, XB, TT, k_evac,
                           t_xb, cast=True, nmu_tok=st["nmu_row"], wkey="gw", nslots=1)
            ps_v = [self.ps["pc"], self.ps["ub"]]
            prev_ev_v = getattr(self, "_v_ev", {})
            evs = {}
            for tp in range(0, KPG, 2):
                pair = [tc for tc in (tp, tp + 1) if tc < KPG]
                t_vmm = {}
                for dc in range(C):
                    slot = dc % 2
                    dep = self.slotdep.get("vw" + str(slot))
                    t_w = self.P.dma("gpsimd", vw[slot][:, 0:V0], wv[dc],
                                     waits=([dep] if dep else []), q=["gwC", "gwD"][slot])
                    for j, tc in enumerate(pair):
                        lhs = XB[:, dc * TT + tc * 128: dc * TT + (tc + 1) * 128]
                        pv = prev_ev_v.get(tc)
                        t_vmm[tc] = self.mm(ps_v[j][:, 0:V0], lhs, vw[slot][:, 0:V0],
                                            start=(dc == 0), stop=False,
                                            waits=[t_w] + t_xb +
                                            ([pv] if pv and dc == 0 else []) +
                                            (self.bw(["pc", "ub"][j]) if dc == 0 else []),
                                            force_inc=(tc == pair[-1]))
                    self.slotdep["vw" + str(slot)] = t_vmm[pair[-1]]
                for j, tc in enumerate(pair):
                    t_vmm[tc] = self.mm(ps_v[j][:, 0:V0],
                                        self.sb["row_bf"][0:1, tc * 128:(tc + 1) * 128],
                                        self.sb["vcs"][0:1, 0:V0], start=False, stop=True,
                                        waits=[self._t_vcs, st["nmu_row"]])
                for j, tc in enumerate(pair):
                    gt = g * KPG + tc
                    dst = v0t[:, gt * V0:(gt + 1) * V0]
                    evs[tc] = self.vop(lambda e, d=dst, p=ps_v[j], tc=tc: e.tensor_scalar(
                        d, p[:, 0:V0], self.sb["rcolk"][:, tc:tc + 1], None, AluOpType.mult),
                        waits=[t_vmm[tc]] + t_rc)
                    self.bs(["pc", "ub"][j], evs[tc])
            self._v_ev = evs
        self.fence()
        self.free("vw0", "vw1")
        self.t_l0kv = self.P.tok("V")

    def l0_vsuf(self):
        self._stream_free()
        c = self.cfg
        NT, KPG, KV0 = c["NT"], c["KPG"], c["KV0"]
        V0 = KV0 * 128
        if NT <= 1:
            self.t_vsuf = None
            return
        vbufs = [self.sbuf(f"vsuf{b}", [1, V0], BF16) for b in range(NT - 1)]
        v0t = self.sb["v0t"]
        v_done = self.P.tok("V")
        t_sc = None
        for b in range(1, NT):
            t_mm = None
            nch = (NT - b) * KPG
            for i, tc in enumerate(range(b * KPG, NT * KPG)):
                t_mm = self.mm(self.ps["pd"][0:1, 0:V0], self.sb["ones_col"],
                               v0t[:, tc * V0:(tc + 1) * V0],
                               start=(i == 0), stop=(i == nch - 1),
                               waits=[v_done] + (self.bw("pd") if i == 0 else []) +
                               ([t_sc] if t_sc and i == 0 else []))
            dst = vbufs[b - 1][0:1, :]
            t_sc = self.vop(lambda e, d=dst: e.tensor_scalar_mul(d, self.ps["pd"][0:1, 0:V0],
                            E50), waits=[t_mm])
        self.bs("pd", t_sc)
        self.t_vsuf = t_sc

    def l0_phaseB(self):
        self.fence()
        self.sbuf("gw1", [128, self.cfg["C"] * 128], BF16)
        c = self.cfg
        S, TT, C, NT, KPG, NH0, KV0 = (c["S"], c["TT"], c["C"], c["NT"], c["KPG"],
                                       c["NH0"], c["KV0"])
        V0 = KV0 * 128
        rep = NH0 // KV0
        HQ = max(1, NH0 // 8)
        X, k0, v0t, XB = self.sb["X"], self.sb["k0"], self.sb["v0t"], self.sb["XB"]
        masks = self.sb["masks0"]
        prs = [self.sb[f"pr{i}"] for i in range(2)]
        wq = self.din("wq_prep", [C, 128, C * 128], F32)
        wo = self.din("wo_prep", [C, 128, C * 128], F32)
        q_q = self.sbuf("q_q", [128, HQ * TT], BF16)
        ctx_q = self.sbuf("ctx_q", [128, HQ * TT], BF16)
        self.sb["rowd"] = self.sb["row_bf2"]
        wbufs = [self.sb["gw0"], self.sb["gw1"]]
        pr_dep = {}
        for g in range(NT):
            st = self.stats_fold(
                lambda cc, g=g: X[:, cc * S + g * TT: cc * S + (g + 1) * TT],
                TT, C, waits=[self.t_x, self.P.tok("T"), self.P.tok("V")], want_rq=True)
            t_xb = []
            tprev = [self.P.tok("T"), self.t_x]
            for cc in range(C):
                t_xb.append(self.vop(lambda e, cc=cc, g=g: e.tensor_copy(
                    XB[:, cc * TT:(cc + 1) * TT],
                    X[:, cc * S + g * TT: cc * S + (g + 1) * TT]), waits=tprev))
            nkc = KPG * (g + 1)
            n_up = S - TT * (g + 1)
            for hq in range(NH0 // HQ):
                heads = list(range(hq * HQ, (hq + 1) * HQ))
                def q_evac(i, psm, t_mm, st=st):
                    dst = q_q[:, i * TT:(i + 1) * TT]
                    return self.vop(lambda e, d=dst, p=psm: e.tensor_mul(
                        d, p[:, 0:TT], self.sb["rq_t"][:, 0:TT]), waits=[t_mm, st["rq_b"]])
                t_q = self.proj_fold(lambda i, heads=heads: wq[heads[i]], HQ, C, XB, TT,
                                     q_evac, t_xb, cast=True, nmu_tok=st["nmu_row"],
                                     wkey="gw")
                for i, h in enumerate(heads):
                    kvh = h // rep
                    qs = q_q[:, i * TT:(i + 1) * TT]
                    t_d_mm = None; t_ctx_mm = None
                    for kc in range(nkc):
                        pb = ["p0", "p1"][kc % 2]
                        psm = self.ps[pb]
                        t_s = self.mm(psm[:, 0:TT],
                                      k0[:, kvh * S + kc * 128: kvh * S + (kc + 1) * 128],
                                      qs, start=True, stop=True,
                                      waits=[t_q[i], self.t_l0kv] + self.bw(pb))
                        last = t_s
                        if kc >= KPG * g:
                            r = kc - KPG * g
                            last = self.vop(lambda e, p=psm, r_=r: e.tensor_add(
                                p[:, 0:TT], p[:, 0:TT], self._mask_slice(r_)),
                                waits=[t_s] + self.t_masks)
                        pr = prs[kc % 2][:, 0:TT]
                        t_u = self.sop(lambda e, p=psm: e.activation(
                            p[:, 0:TT], p[:, 0:TT], AF.Relu, scale=-1.0,
                            bias=self.sb["c88_col"]), waits=[last, self.t_setup])
                        t_e = self.sop(lambda e, o=pr, p=psm: e.activation(
                            o, p[:, 0:TT], AF.Exp, scale=-1.0, bias=self.sb["c88_col"]),
                            waits=[t_u] + ([pr_dep[kc % 2]] if kc % 2 in pr_dep else []))
                        self.bs(pb, t_e)
                        t_cl = self.vop(lambda e, o=pr: e.tensor_scalar(o, o, E50, EP50,
                                        AluOpType.max, AluOpType.min), waits=[t_e])
                        t_d_mm = self.mm(self.ps["pd"][0:1, 0:TT], self.sb["ones_col"], pr,
                                         start=(kc == 0), stop=(kc == nkc - 1),
                                         waits=[t_cl] + (self.bw("pd") if kc == 0 else []))
                        t_ctx_mm = self.mm(self.ps["pc"][:, 0:TT],
                                           v0t[:, kc * V0 + kvh * 128: kc * V0 + (kvh + 1) * 128],
                                           pr, start=(kc == 0),
                                           stop=(kc == nkc - 1 and n_up == 0),
                                           waits=[t_cl] + (self.bw("pc") if kc == 0 else []))
                        pr_dep[kc % 2] = t_ctx_mm
                    if n_up > 0:
                        t_ctx_mm = self.mm(self.ps["pc"][:, 0:TT],
                                           self.sb[f"vsuf{g}"][0:1, kvh * 128:(kvh + 1) * 128],
                                           self.sb["ones_rowT"], start=False, stop=True,
                                           waits=[self.t_vsuf])
                    d_const = float(n_up) * E50
                    t_r1 = self.vop(lambda e, cst=d_const: e.tensor_scalar_add(
                        self.ps["u1"][0:1, 0:TT], self.ps["pd"][0:1, 0:TT], cst),
                        waits=[t_d_mm] + self.bw("u1"))
                    self.bs("pd", t_r1)
                    t_r2 = self.vop(lambda e: e.reciprocal(self.sb["rowd"][0:1, 0:TT],
                                    self.ps["u1"][0:1, 0:TT]),
                                    waits=[t_r1, self.bankdep.get("rowd")])
                    self.bs("u1", t_r2)
                    t_b = self.mm(self.ps["ub"][:, 0:TT], self.sb["ones_row"],
                                  self.sb["rowd"][0:1, 0:TT], start=True, stop=True,
                                  waits=[t_r2] + self.bw("ub"))
                    self.bs("rowd", t_b)
                    dslot = nkc % 2
                    dbc = prs[dslot]
                    t_be = self.vop(lambda e, dbc=dbc: e.tensor_copy(dbc[:, 0:TT],
                                    self.ps["ub"][:, 0:TT]),
                                    waits=[t_b] + ([pr_dep[dslot]] if dslot in pr_dep else []))
                    self.bs("ub", t_be)
                    dst = ctx_q[:, i * TT:(i + 1) * TT]
                    t_cev = self.vop(lambda e, d=dst, dbc=dbc: e.tensor_mul(
                        d, self.ps["pc"][:, 0:TT], dbc[:, 0:TT]), waits=[t_ctx_mm, t_be])
                    self.bs("pc", t_cev)
                    pr_dep[dslot] = t_cev
                t_ctx_all = self.P.tok("V")
                for oc in range(C):
                    slot = oc % 2
                    dep = self.slotdep.get("gw" + str(slot))
                    t_w = self.P.dma("gpsimd", wbufs[slot][:, 0:HQ * 128],
                                     wo[oc][:, heads[0] * 128:(heads[0] + HQ) * 128],
                                     waits=([dep] if dep else []), q=["gwA", "gwB"][slot])
                    pb = ["p0", "p1"][oc % 2]
                    psm = self.ps[pb]
                    t_mm = None
                    for i in range(HQ):
                        t_mm = self.mm(psm[:, 0:TT], wbufs[slot][:, i * 128:(i + 1) * 128],
                                       ctx_q[:, i * TT:(i + 1) * TT],
                                       start=(i == 0), stop=(i == HQ - 1),
                                       waits=[t_w, t_ctx_all] + (self.bw(pb) if i == 0 else []))
                    self.slotdep["gw" + str(slot)] = t_mm
                    dst = X[:, oc * S + g * TT: oc * S + (g + 1) * TT]
                    t_ev = self.vop(lambda e, d=dst, p=psm: e.tensor_add(d, d, p[:, 0:TT]),
                                    waits=[t_mm])
                    self.bs(pb, t_ev)
        self.fence()
        self._dbg_ctx = ctx_q
        self._dbg_q = q_q
        self.free("q_q", "ctx_q")
        self.t_l0attn = self.P.tok("V")

    def absmean(self, wprep, n_oc, n_cols, acc_name, tilew=None):
        tilew = tilew or self.SWW
        sw = [self.sb["sw0"], self.sb["sw1"]]
        if "am_acc" not in self.sb:
            self.sbuf("am_acc", [128, 1], F32, small=True)
            self.sbuf("am_s", [1, 1], F32, small=True)
            self.sbuf("am_p", [128, 1], F32, small=True)
        acc, accs, part = self.sb["am_acc"], self.sb["am_s"], self.sb["am_p"]
        t_z = self.vop(lambda e: e.memset(acc, 0.0), waits=[self.bankdep.get("am")])
        t_prev = None
        i = 0
        for oc in range(n_oc):
            for a0 in range(0, n_cols, tilew):
                n = min(tilew, n_cols - a0)
                slot = i % 2; i += 1
                dep = self.slotdep.get("sw" + str(slot))
                t_w = self.P.dma("sync", sw[slot][:, 0:n], wprep[oc][:, a0:a0 + n],
                                 waits=([dep] if dep else []), q=["swA", "swB"][slot])
                t_r = self.vop(lambda e, s=sw[slot], n=n: e.reduce_sum(part, s[:, 0:n], AX.X,
                               apply_absolute_value=True),
                               waits=[t_w] + ([t_prev] if t_prev else []))
                self.slotdep["sw" + str(slot)] = t_r
                t_prev = self.vop(lambda e: e.tensor_add(acc, acc, part), waits=[t_r, t_z])
        t_t = self.mm(self.ps["u0"][0:1, 0:1], self.sb["ones_colf"], acc, start=True,
                      stop=True, waits=[t_prev] + self.bw("u0"))
        t_s = self.vop(lambda e: e.tensor_copy(accs, self.ps["u0"][0:1, 0:1]), waits=[t_t])
        self.bs("u0", t_s)
        self.bs("am", t_s)
        return t_s, accs

    def tau_cols(self, t_sum, sum_ap, coef, name, clip_min=None):
        if "tau_sh_r" not in self.sb:
            self.sbuf("tau_sh_r", [1, 1], F32, small=True)
        tau_r = self.sb["tau_sh_r"]
        t1 = self.vop(lambda e: e.tensor_scalar_mul(tau_r, sum_ap, coef), waits=[t_sum])
        if clip_min is not None:
            t1 = self.vop(lambda e, cm=clip_min: e.tensor_scalar_max(tau_r, tau_r, cm),
                          waits=[t1])
        if "tau_sh_c" not in self.sb:
            self.sbuf("tau_sh_c", [128, 1], F32, small=True)
            self.sbuf("tau_sh_nc", [128, 1], F32, small=True)
        tau_c, ntau_c = self.sb["tau_sh_c"], self.sb["tau_sh_nc"]
        t1 = self.P.op("vector", lambda e: e.nop(),
                       waits=[t1, self.bankdep.get("tauc")], inc="V")
        t3 = self.mm(self.ps["u2"][:, 0:1], self.sb["ones_rowf"], tau_r, start=True,
                     stop=True, waits=[t1] + self.bw("u2"))
        t4 = self.vop(lambda e: e.tensor_copy(tau_c, self.ps["u2"][:, 0:1]), waits=[t3])
        t5 = self.vop(lambda e: e.tensor_scalar_mul(ntau_c, self.ps["u2"][:, 0:1], -1.0),
                      waits=[t3])
        self.bs("u2", t5)
        return [t4, t5], tau_c, ntau_c

    def ternarize(self, wprep, scratch, n_oc, n_cols, tau_toks, tau_c, ntau_c, tilew=None):
        tilew = tilew or self.SWW
        pass
        sw = [self.sb["sw0"], self.sb["sw1"]]
        tb = self.sb["tn0"]
        tt_ = self.sb["tern_t"]
        i = 0
        prev3 = None
        for oc in range(n_oc):
            for a0 in range(0, n_cols, tilew):
                n = min(tilew, n_cols - a0)
                slot = i % 2; i += 1
                dep = self.slotdep.get("sw" + str(slot))
                t_w = self.P.dma("sync", sw[slot][:, 0:n], wprep[oc][:, a0:a0 + n],
                                 waits=([dep] if dep else []), q=["swA", "swB"][slot])
                depT = self.slotdep.get("tn0")
                t1 = self.sop(lambda e, s=sw[slot], n=n: e.activation(
                    tb[:, 0:n], s[:, 0:n], AF.Sign, bias=ntau_c),
                    waits=[t_w] + tau_toks + ([depT] if depT else []))
                t2 = self.sop(lambda e, s=sw[slot], n=n: e.activation(
                    tt_[:, 0:n], s[:, 0:n], AF.Sign, bias=tau_c),
                    waits=[t_w] + tau_toks + ([prev3] if prev3 else []))
                self.slotdep["sw" + str(slot)] = t2
                t3 = self.vop(lambda e, n=n: e.tensor_add(tb[:, 0:n], tb[:, 0:n],
                              tt_[:, 0:n]), waits=[t1, t2])
                prev3 = t3
                t_st = self.P.dma("sync", scratch[oc][:, a0:a0 + n], tb[:, 0:n],
                                  waits=[t3], q=["qscw", "qscx"][i % 2])
                self.slotdep["tn0"] = t_st
        self.bs("tauc", self.P.tok("S"))
        return [self.P.tok("qscw"), self.P.tok("qscx")]

    def _stream_views(self):
        w = min(512, (self.cfg["C"] * self.cfg["TT"]) // 6 // 64 * 64)
        assert w >= 64
        self.SWW = w
        i = self._sv_id
        self._sv_id += 1
        self.sb["sw0"] = self.sbuf(f"sw0_{i}", [128, w], F32)
        self.sb["sw1"] = self.sbuf(f"sw1_{i}", [128, w], F32)
        self.sb["tn0"] = self.sbuf(f"tn0_{i}", [128, w], BF16)
        self.sb["tern_t"] = self.sbuf(f"tt_{i}", [128, w], BF16)

    def _stream_free(self):
        i = self._sv_id - 1
        self.fence()
        self.free(f"sw0_{i}", f"sw1_{i}", f"tn0_{i}", f"tt_{i}")

    def l0_wffn_prep(self):
        c = self.cfg
        C, CF0 = c["C"], c["CF0"]
        self._stream_views()
        self.w1p = self.din("w1_prep", [CF0, 128, C * 128], F32)
        self.w2p = self.din("w2_prep", [C, 128, CF0 * 128], F32)
        self.w1q_s = self.dscr("w1q_s", [CF0, 128, C * 128], BF16)
        self.w2q_s = self.dscr("w2q_s", [C, 128, CF0 * 128], BF16)
        t1, s1 = self.absmean(self.w1p, CF0, C * 128, "am1")
        tt1, tau1, ntau1 = self.tau_cols(t1, s1, 0.7 / (c["F0"] * c["D"]), "tau1")
        self.t_w1scr = self.ternarize(self.w1p, self.w1q_s, CF0, C * 128, tt1, tau1, ntau1)

    def l0_wffn_prep2(self):
        c = self.cfg
        C, CF0 = c["C"], c["CF0"]
        self._stream_views()
        t2, s2 = self.absmean(self.w2p, C, CF0 * 128, "am2")
        tt2, tau2, ntau2 = self.tau_cols(t2, s2, 0.7 / (c["F0"] * c["D"]), "tau2")
        self.t_w2scr = self.ternarize(self.w2p, self.w2q_s, C, CF0 * 128, tt2, tau2, ntau2)

    def stats_plain(self, xsrc, n, nchunks, waits, inv_dim=None):
        st = self.stats_fold(xsrc, n, nchunks, waits, want_rq=False, want_mu_b=True,
                             inv_dim=inv_dim)
        return st

    def ln_apply2(self, dst, xsrc, waits):
        t1 = self.vop(lambda e: e.tensor_add(dst, xsrc, self.sb["mu_t"][:, 0:dst.shape[1]]),
                      waits=waits)
        t2 = self.vop(lambda e: e.tensor_mul(dst, dst, self.sb["r_t"][:, 0:dst.shape[1]]),
                      waits=[t1])
        return t2

    def l0_ffn(self):
        c = self.cfg
        S, TT, C, NT, CF0 = c["S"], c["TT"], c["C"], c["NT"], c["CF0"]
        X = self.sb["X"]
        NQ = CF0 // 4
        lnx = self.RGA
        h_q = self.sbuf("h_q0", [128, NQ * TT], BF16)
        self.sbuf("w2b0", [128, NQ * 128], BF16)
        self.sbuf("w2b1", [128, NQ * 128], BF16)
        b1c = self.din("b1_cols", [128, CF0], F32)
        b2r = self.din("b2_row", [1, C * 128], F32)
        b1s = self.sbuf("b1c", [128, CF0], F32, small=(CF0 * 4 <= 1024))
        b2f = self.sbuf("b2f", [1, 128], F32, small=True)
        b2s = self.sbuf("b2s", [1, 128], BF16, small=True)
        self.P.dma("sync", b1s, b1c, q="qx")
        t_b1 = self.P.tok("qx")
        t_b2 = None
        w1b = [self.sb["gw0"], self.sb["gw1"]]
        w2b = [self.sb["w2b0"], self.sb["w2b1"]]
        self.seed_slot("w1q0", "gw0"); self.seed_slot("w1q1", "gw1")
        for ti in range(NT):
            st = self.stats_plain(
                lambda cc, ti=ti: X[:, cc * S + ti * TT: cc * S + (ti + 1) * TT],
                TT, C, waits=[self.P.tok("T"), self.P.tok("V")])
            t_ln = []
            for cc in range(C):
                t_ln.append(self.ln_apply2(lnx[:, cc * TT:(cc + 1) * TT],
                                           X[:, cc * S + ti * TT: cc * S + (ti + 1) * TT],
                                           waits=[st["nmu_b"], st["r_b"], self.P.tok("T")]))
            for fq in range(4):
                t_h = []
                for j in range(NQ):
                    fc = fq * NQ + j
                    slot = fc % 2
                    dep = self.slotdep.get("w1q" + str(slot))
                    t_w = self.P.dma("sync", w1b[slot][:, 0:C * 128], self.w1q_s[fc],
                                     waits=([dep] if dep else []) + self.t_w1scr,
                                     q=["swA", "swB"][slot])
                    pb = ["p0", "p1"][j % 2]
                    psm = self.ps[pb]
                    t_mm = None
                    for dc in range(C):
                        t_mm = self.mm(psm[:, 0:TT], w1b[slot][:, dc * 128:(dc + 1) * 128],
                                       lnx[:, dc * TT:(dc + 1) * TT],
                                       start=(dc == 0), stop=(dc == C - 1),
                                       waits=[t_w] + t_ln + (self.bw(pb) if dc == 0 else []))
                    self.slotdep["w1q" + str(slot)] = t_mm
                    t_g = self.sop(lambda e, p=psm, j=j, fc=fc: e.activation(
                        h_q[:, j * TT:(j + 1) * TT], p[:, 0:TT], AF.Gelu, scale=0.5,
                        bias=b1s[:, fc:fc + 1]), waits=[t_mm, t_b1])
                    self.bs(pb, t_g)
                    t_h.append(t_g)
                for oc in range(C):
                    slot = oc % 2
                    dep = self.slotdep.get("w2q" + str(slot))
                    t_w = self.P.dma("sync", w2b[slot][:, 0:NQ * 128],
                                     self.w2q_s[oc][:, fq * NQ * 128:(fq + 1) * NQ * 128],
                                     waits=([dep] if dep else []) + self.t_w2scr,
                                     q=["swC", "swD"][slot])
                    pb = ["p0", "p1"][oc % 2]
                    psm = self.ps[pb]
                    t_mm = None
                    for j in range(NQ):
                        t_mm = self.mm(psm[:, 0:TT], w2b[slot][:, j * 128:(j + 1) * 128],
                                       h_q[:, j * TT:(j + 1) * TT],
                                       start=(j == 0), stop=False,
                                       waits=[t_w] + t_h + (self.bw(pb) if j == 0 else []))
                    if fq == 3:
                        t_bl = self.P.dma("sync", b2f, b2r[0:1, oc * 128:(oc + 1) * 128],
                                          waits=[t_b2] if t_b2 else [], q="qx")
                        t_b2 = self.vop(lambda e: e.tensor_scalar_mul(b2s, b2f, 2.0),
                                        waits=[t_bl, self.P.tok("T")])
                        t_mm = self.mm(psm[:, 0:TT], b2s, self.sb["ones_rowT"],
                                       start=False, stop=True, waits=[t_b2])
                    else:
                        t_mm = self.mm(psm[:, 0:TT], self.sb["zrow"], self.sb["ones_rowT"],
                                       start=False, stop=True, waits=[self.t_setup])
                    self.slotdep["w2q" + str(slot)] = t_mm
                    dst = X[:, oc * S + ti * TT: oc * S + (ti + 1) * TT]
                    t_ev = self.vop(lambda e, d=dst, p=psm: e.scalar_tensor_tensor(
                        d, p[:, 0:TT], 0.5, d, AluOpType.mult, AluOpType.add),
                        waits=[t_mm])
                    self.bs(pb, t_ev)
        self._stream_free()
        self.free("w2b0", "w2b1", "h_q0")
        self.t_l0 = self.P.tok("V")

    def l1_weights(self):
        c = self.cfg
        C = c["C"]
        V1 = c["KV1"] * 64
        KC1 = max(1, V1 // 128)
        wqp = self.din("l1wq_prep", [C, 128, C * 128], F32)
        wkp = self.din("l1wk_prep", [KC1, 128, C * 128], F32)
        wvp = self.din("l1wv_prep", [C, 128, V1], F32)
        wop = self.din("l1wo_prep", [C, 128, C * 128], F32)
        self.l1wq_s = self.dscr("l1wq_s", [C, 128, C * 128], BF16)
        self.l1wk_s = self.dscr("l1wk_s", [KC1, 128, C * 128], BF16)
        self.l1wv_s = self.dscr("l1wv_s", [C, 128, V1], BF16)
        self.l1wo_s = self.dscr("l1wo_s", [C, 128, C * 128], BF16)
        self._stream_views()
        for nm, wp, scr, noc, ncols in (
                ("q1", wqp, self.l1wq_s, C, C * 128), ("k1", wkp, self.l1wk_s, KC1, C * 128),
                ("v1", wvp, self.l1wv_s, C, V1), ("o1", wop, self.l1wo_s, C, C * 128)):
            t, s = self.absmean(wp, noc, ncols, "am_" + nm)
            cnt = noc * 128 * ncols
            tt_, tau, ntau = self.tau_cols(t, s, 0.5 / cnt, "tau_" + nm, clip_min=0.5e-5)
            self.ternarize(wp, scr, noc, ncols, tt_, tau, ntau)
        self.t_l1w = [self.P.tok("qscw"), self.P.tok("qscx")]
        self._stream_free()

    def quant_chain(self, src, dst, n, nchunks, waits):
        if "m1q_t" not in self.sb:
            self.sbuf("m1q_t", [128, 512], F32)
            self.sbuf("xsf", [128, 512], F32)
            self.sbuf("sgnq", [128, 512], BF16)
            self.sbuf("qint", [128, 512], I32)
            self.sbuf("scl_b", [128, 512], BF16)
            self.sbuf("rscl_b", [128, 512], BF16)
        m1, xsf, sgn, qi = (self.sb[k] for k in ("m1q_t", "xsf", "sgnq", "qint"))
        scl, rscl = self.sb["scl_b"], self.sb["rscl_b"]
        st = self.stats_plain(src, n, nchunks, waits)
        cch = self.sb["cch"]
        t_mx = None
        t_ln0 = None
        for cc in range(nchunks):
            t1 = self.ln_apply2(cch[:, 0:n], src(cc),
                                waits=list(waits) + [st["nmu_b"], st["r_b"]] +
                                ([t_mx] if t_mx else []))
            if cc == 0:
                t_mx = self.vop(lambda e: e.scalar_tensor_tensor(
                    m1[:, 0:n], cch[:, 0:n], -1.0, cch[:, 0:n],
                    AluOpType.mult, AluOpType.max), waits=[t1])
            else:
                t_ab = self.vop(lambda e: e.scalar_tensor_tensor(
                    xsf[:, 0:n], cch[:, 0:n], -1.0, cch[:, 0:n],
                    AluOpType.mult, AluOpType.max), waits=[t1, t_mx])
                t_mx = self.vop(lambda e: e.tensor_max(
                    m1[:, 0:n], m1[:, 0:n], xsf[:, 0:n]), waits=[t_ab])
        nt128 = n // 128
        t_rows = []
        t_prev = None
        for tc in range(nt128):
            t_t = self.P.op("tensor", lambda e, tc=tc: e.transpose(
                self.ps["u2"][:, 0:128], m1[:, tc * 128:(tc + 1) * 128], self.sb["identf"]),
                waits=[t_mx] + self.bw("u2"), inc="T")
            t_r = self.vop(lambda e: e.tensor_reduce(self.sb["mcolq"], self.ps["u2"][:, 0:128],
                           AX.X, AluOpType.max), waits=[t_t] + ([t_prev] if t_prev else []))
            self.bs("u2", t_r)
            t_t2 = self.P.op("tensor", lambda e: e.transpose(
                self.ps["u1"][0:1, 0:128], self.sb["mcolq"], self.sb["identf"]),
                waits=[t_r] + self.bw("u1"), inc="T")
            t_prev = self.vop(lambda e, tc=tc: e.tensor_copy(
                self.ps["pd"][0:1, tc * 128:(tc + 1) * 128], self.ps["u1"][0:1, 0:128]),
                waits=[t_t2] + (self.bw("pd") if tc == 0 else []))
            self.bs("u1", t_prev)
            t_rows.append(t_prev)
        t_c1 = self.vop(lambda e: e.tensor_scalar_max(self.ps["u1"][0:1, 0:n],
                        self.ps["pd"][0:1, 0:n], 1e-5),
                        waits=t_rows + self.bw("u1"))
        self.bs("pd", t_c1)
        t_c2 = self.vop(lambda e: e.reciprocal(self.ps["pd"][0:1, 0:n],
                        self.ps["u1"][0:1, 0:n]), waits=[t_c1])
        t_c3 = self.vop(lambda e: e.tensor_scalar_mul(self.sb["row_bf"][0:1, 0:n],
                        self.ps["pd"][0:1, 0:n], 127.0), waits=[t_c2])
        t_c4 = self.vop(lambda e: e.tensor_scalar_mul(self.sb["row_bf2"][0:1, 0:n],
                        self.ps["u1"][0:1, 0:n], 1.0 / 127.0), waits=[t_c1, t_c2])
        self.bs("pd", t_c3)
        self.bs("u1", t_c4)
        tb1 = self.mm(self.ps["u2"][:, 0:n], self.sb["ones_row"], self.sb["row_bf"][0:1, 0:n],
                      start=True, stop=True, waits=[t_c3] + self.bw("u2"))
        te1 = self.vop(lambda e: e.tensor_copy(scl[:, 0:n], self.ps["u2"][:, 0:n]),
                       waits=[tb1])
        tb2 = self.mm(self.ps["u2"][:, 0:n], self.sb["ones_row"], self.sb["row_bf2"][0:1, 0:n],
                      start=True, stop=True, waits=[te1, t_c4])
        te2 = self.vop(lambda e: e.tensor_copy(rscl[:, 0:n], self.ps["u2"][:, 0:n]),
                       waits=[tb2])
        self.bs("u2", te2)
        toks = []
        for cc in range(nchunks):
            t1a = self.ln_apply2(cch[:, 0:n], src(cc),
                                 waits=list(waits) + [st["nmu_b"], st["r_b"]] + toks[-1:] +
                                 ([t_mx] if cc == 0 else []))
            t1 = self.vop(lambda e: e.tensor_tensor(xsf[:, 0:n], cch[:, 0:n],
                          scl[:, 0:n], AluOpType.mult), waits=[te1, t1a])
            t2 = self.sop(lambda e: e.activation(sgn[:, 0:n], xsf[:, 0:n], AF.Sign),
                          waits=[t1])
            t3 = self.vop(lambda e: e.scalar_tensor_tensor(xsf[:, 0:n], sgn[:, 0:n], 0.5,
                          xsf[:, 0:n], AluOpType.mult, AluOpType.add), waits=[t2])
            t4 = self.vop(lambda e: e.tensor_copy(qi[:, 0:n], xsf[:, 0:n]), waits=[t3])
            t5 = self.vop(lambda e, d=dst(cc): e.tensor_tensor(d, qi[:, 0:n],
                          rscl[:, 0:n], AluOpType.mult), waits=[t4, te2])
            toks.append(t5)
        return toks

    def l1_qslot(self, h):
        c = self.cfg
        rep1 = c["NH1"] // c["KV1"]
        par = (h // rep1) % 2
        same = [x for x in range(c["NH1"]) if ((x // rep1) % 2) == par]
        return same.index(h), 64 * par

    def l1_main(self, upto=99):
        c = self.cfg
        S, TT, C, NT, KPG = c["S"], c["TT"], c["C"], c["NT"], c["KPG"]
        NH1, KV1 = c["NH1"], c["KV1"]
        rep1 = NH1 // KV1
        V1 = KV1 * 64
        V1A = KV1 * 65
        KC1 = max(1, V1 // 128)
        X = self.sb["X"]
        masks = self.sb["masks0"]
        prs = [self.sb[f"pr{i}"] for i in range(2)]
        drow0 = self.sbuf("drow0", [1, 512], BF16)
        stg = self.sbuf("stg", [64, 512], BF16)
        rmsc = self.din("rms_cols", [128, C], F32)
        rmss = self.sbuf("rmsc", [128, C], F32, small=True)
        self.P.dma("sync", rmss, rmsc, q="qx")
        t_rms = self.P.tok("qx")
        if "gw1" in self._live:
            self.fence()
            self.free("gw1")
        k1 = self.sbuf("k1", [128, KC1 * S], BF16)
        v1t = self.sbuf("v1t", [128, (S // 128) * V1A], BF16)
        WV = (S // 128) * V1A
        for a0 in range(0, WV, V1A * 4):
            n = min(V1A * 4, WV - a0)
            self.gop(lambda e, a0=a0, n=n: e.memset(v1t[:, a0:a0 + n], 1.0))
        t_vm = self.P.tok("G")
        wtb = [self.sb["gw0"]]
        self.seed_slot("l1w0", "w1q0"); self.seed_slot("l1w1", "w1q1")
        pr_dep = {}
        if getattr(self, "_dbg_op", 0) == 1:
            self.fence()
            return
        for g in range(NT):
            Y = self.RGA
            st = self.stats_plain(
                lambda cc, g=g: X[:, cc * S + g * TT: cc * S + (g + 1) * TT],
                TT, C, waits=[self.P.tok("T"), self.P.tok("V")])
            if getattr(self, "_dbg_op", 0) == 2:
                self.fence()
                return
            t_y = []
            for cc in range(C):
                xs = X[:, cc * S + g * TT: cc * S + (g + 1) * TT]
                tmp = self.sb["cch"]
                t1 = self.vop(lambda e, a=xs: e.tensor_add(tmp[:, 0:TT], a,
                              self.sb["mu_t"][:, 0:TT]),
                              waits=[st["nmu_b"]] + t_y[-1:])
                t2 = self.vop(lambda e: e.tensor_mul(self.sb["pr0"][:, 0:TT], tmp[:, 0:TT],
                              self.sb["r_t"][:, 0:TT]),
                              waits=[t1, st["r_b"]] + ([self.P.tok("S"), self.P.tok("T")] if cc == 0 else []))
                t3 = self.vop(lambda e, d=xs: e.tensor_add(d, d, self.sb["pr0"][:, 0:TT]),
                              waits=[t2])
                t4 = self.vop(lambda e, cc=cc: e.scalar_tensor_tensor(
                    Y[:, cc * TT:(cc + 1) * TT], tmp[:, 0:TT], rmss[:, cc:cc + 1],
                    self.sb["r_t"][:, 0:TT], AluOpType.mult, AluOpType.mult),
                    waits=[t1, t_rms, t3])
                t_y.append(t4)
            if upto <= 0:
                self.fence()
                return
            t_xq = self.quant_chain(lambda cc: Y[:, cc * TT:(cc + 1) * TT],
                                    lambda cc: Y[:, cc * TT:(cc + 1) * TT],
                                    TT, C, waits=t_y)
            xq = Y
            if upto <= 1:
                self.fence()
                return
            self.fence()
            if "q1g" not in self.sb:
                self.sbuf("q1g", [128, C * TT], BF16)
            q1g = self.sb["q1g"]
            for oc in range(KC1):
                slot = oc % len(wtb)
                dep = self.slotdep.get("l1w" + str(slot))
                t_w = self.P.dma("sync", wtb[slot][:, 0:C * 128], self.l1wk_s[oc],
                                 waits=([dep] if dep else []) + self.t_l1w,
                                 q=["swC", "swD"][slot])
                pb = ["p0", "p1"][oc % 2]
                psm = self.ps[pb]
                t_mm = None
                for dc in range(C):
                    t_mm = self.mm(psm[:, 0:TT], wtb[slot][:, dc * 128:(dc + 1) * 128],
                                   xq[:, dc * TT:(dc + 1) * TT], start=(dc == 0),
                                   stop=(dc == C - 1),
                                   waits=[t_w] + t_xq + (self.bw(pb) if dc == 0 else []))
                self.slotdep["l1w" + str(slot)] = t_mm
                dst = k1[:, oc * S + g * TT: oc * S + (g + 1) * TT]
                t_ev = self.vop(lambda e, d=dst, p=psm: e.tensor_scalar_mul(d, p[:, 0:TT], 0.5),
                                waits=[t_mm])
                self.bs(pb, t_ev)
            ps_v = [self.ps["pc"], self.ps["ub"]]
            prev_ev_v = getattr(self, "_v1_ev", {})
            evs = {}
            for tp in range(0, KPG, 2):
                pair = [tc for tc in (tp, tp + 1) if tc < KPG]
                t_vmm = {}
                for dc in range(C):
                    slot = dc % len(wtb)
                    dep = self.slotdep.get("l1w" + str(slot))
                    t_w = self.P.dma("sync", wtb[slot][:, 0:V1], self.l1wv_s[dc],
                                     waits=([dep] if dep else []) + self.t_l1w,
                                     q=["swC", "swD"][slot])
                    for j, tc in enumerate(pair):
                        lhs = xq[:, dc * TT + tc * 128: dc * TT + (tc + 1) * 128]
                        pv = prev_ev_v.get(tc)
                        t_vmm[tc] = self.mm(ps_v[j][:, 0:V1], lhs, wtb[slot][:, 0:V1],
                                            start=(dc == 0), stop=(dc == C - 1),
                                            waits=[t_w] + t_xq +
                                            ([pv] if pv and dc == 0 else []) +
                                            (self.bw(["pc", "ub"][j]) if dc == 0 else []),
                                            force_inc=(tc == pair[-1]))
                    self.slotdep["l1w" + str(slot)] = t_vmm[pair[-1]]
                for j, tc in enumerate(pair):
                    gt = g * KPG + tc
                    dst = v1t[:, gt * V1A:(gt + 1) * V1A].rearrange(
                        "p (a b) -> p a b", b=65)[:, :, 0:64]
                    srcv = ps_v[j][:, 0:V1].rearrange("p (a b) -> p a b", b=64)
                    evs[tc] = self.vop(lambda e, d=dst, s=srcv: e.tensor_scalar_mul(d, s, 0.5),
                                       waits=[t_vmm[tc], t_vm])
                    self.bs(["pc", "ub"][j], evs[tc])
            self._v1_ev = evs
            for oc in range(C):
                slot = oc % len(wtb)
                dep = self.slotdep.get("l1w" + str(slot))
                t_w = self.P.dma("sync", wtb[slot][:, 0:C * 128], self.l1wq_s[oc],
                                 waits=([dep] if dep else []) + self.t_l1w,
                                 q=["swC", "swD"][slot])
                pb = ["p0", "p1"][oc % 2]
                psm = self.ps[pb]
                t_mm = None
                for dc in range(C):
                    t_mm = self.mm(psm[:, 0:TT], wtb[slot][:, dc * 128:(dc + 1) * 128],
                                   xq[:, dc * TT:(dc + 1) * TT], start=(dc == 0),
                                   stop=(dc == C - 1),
                                   waits=[t_w] + t_xq + (self.bw(pb) if dc == 0 else []))
                self.slotdep["l1w" + str(slot)] = t_mm
                dst = q1g[:, oc * TT:(oc + 1) * TT]
                t_ev = self.vop(lambda e, d=dst, p=psm: e.tensor_scalar_mul(
                    d, p[:, 0:TT], 0.0625), waits=[t_mm])
                self.bs(pb, t_ev)
            t_qa = self.P.tok("V")
            t_kv = self.P.tok("V")
            if upto <= 2:
                self.fence()
                return
            self.fence()
            ctx1 = self.RGA[:, 0:(NH1 // 2) * TT]
            nheads = getattr(self, "_dbg_nheads", None) or (NH1 if upto >= 6 else 2)
            for h in range(nheads):
                kvh = h // rep1
                qch, qrow = self.l1_qslot(h)
                krow = 64 * (kvh % 2); kch = kvh // 2
                qs_all = q1g[qrow:qrow + 64, qch * TT:(qch + 1) * TT]
                t_mc = []
                mcols = self.sb["rcolk"]
                for j in range(KPG):
                    kext = 128 * (g * KPG + j + 1)
                    qs = q1g[qrow:qrow + 64, qch * TT + j * 128: qch * TT + (j + 1) * 128]
                    t_red_prev = None
                    n512 = (kext + 511) // 512
                    for i in range(n512):
                        n = min(512, kext - 512 * i)
                        pb = ["p0", "p1"][i % 2]
                        psm = self.ps[pb]
                        t_s = self.mm(psm[:, 0:n], qs,
                                      k1[krow:krow + 64, kch * S + 512 * i: kch * S + 512 * i + n],
                                      start=True, stop=True,
                                      waits=[t_qa, t_kv] + self.bw(pb))
                        last = t_s
                        if 512 * i + n == kext:
                            last = self.vop(lambda e, p=psm, n=n: e.tensor_add(
                                p[:, n - 128:n], p[:, n - 128:n], self.sb["mdiagq"]),
                                waits=[t_s, self.t_mdq])
                        if i == 0:
                            t_r = self.vop(lambda e, p=psm, n=n, j=j: e.tensor_reduce(
                                mcols[:, j:j + 1], p[:, 0:n], AX.X, AluOpType.max),
                                waits=[last] + ([t_red_prev] if t_red_prev else []))
                        else:
                            t_r0 = self.vop(lambda e, p=psm, n=n: e.tensor_reduce(
                                self.sb["mcolq"], p[:, 0:n], AX.X, AluOpType.max),
                                waits=[last] + ([t_red_prev] if t_red_prev else []) +
                                [self.P.tok("V")])
                            t_r = self.vop(lambda e, j=j: e.tensor_max(
                                mcols[:, j:j + 1], mcols[:, j:j + 1], self.sb["mcolq"]),
                                waits=[t_r0])
                        t_red_prev = t_r
                        self.bs(pb, t_r)
                        t_mc.append(t_r)
                t_mr = []
                for j in range(KPG):
                    t_t = self.mm(self.ps["u1"][0:1, 0:128], mcols[:, j:j + 1],
                                  self.sb["identf"], start=True, stop=True,
                                  waits=t_mc + self.bw("u1") + self.t_ident)
                    t_e = self.vop(lambda e, j=j: e.tensor_scalar_mul(
                        drow0[0:1, j * 128:(j + 1) * 128],
                        self.ps["u1"][0:1, 0:128], -1.0), waits=[t_t])
                    self.bs("u1", t_e)
                    t_mr.append(t_e)
                if upto <= 3:
                    continue
                nkc = KPG * (g + 1)
                t_ctx_mm = None
                for kc in range(nkc):
                    pb = ["p0", "p1"][kc % 2]
                    psm = self.ps[pb]
                    self.mm(psm[:, 0:TT],
                            k1[krow:krow + 64, kch * S + kc * 128: kch * S + (kc + 1) * 128],
                            qs_all, start=True, stop=False,
                            waits=[t_qa, t_kv] + self.bw(pb))
                    t_s = self.mm(psm[:, 0:TT], self.sb["ones_row"],
                                  drow0[0:1, 0:TT], start=False, stop=True,
                                  waits=t_mr)
                    last = t_s
                    if kc >= KPG * g:
                        r = kc - KPG * g
                        last = self.vop(lambda e, p=psm, r_=r: e.tensor_add(
                            p[:, 0:TT], p[:, 0:TT], self._mask_slice(r_)),
                            waits=[t_s] + self.t_masks)
                    pr = prs[kc % 2][:, 0:TT]
                    t_e = self.sop(lambda e, o=pr, p=psm: e.activation(o, p[:, 0:TT], AF.Exp),
                                   waits=[last] + ([pr_dep[kc % 2]] if kc % 2 in pr_dep else []))
                    self.bs(pb, t_e)
                    t_ctx_mm = self.mm(self.ps["pc"][0:65, 0:TT],
                                       v1t[:, kc * V1A + kvh * 65: kc * V1A + (kvh + 1) * 65],
                                       pr, start=(kc == 0), stop=(kc == nkc - 1),
                                       waits=[t_e] + (self.bw("pc") if kc == 0 else []))
                    pr_dep[kc % 2] = t_ctx_mm
                if upto <= 4:
                    dslot = nkc % 2
                    t_tmp = self.vop(lambda e, ds=dslot: e.tensor_scalar_mul(
                        prs[ds][0:64, 0:TT], self.ps["pc"][0:64, 0:TT], 1.0),
                        waits=[t_ctx_mm] + ([pr_dep[dslot]] if dslot in pr_dep else []))
                    self.bs("pc", t_tmp)
                    pr_dep[dslot] = t_tmp
                    continue
                dslot = nkc % 2
                dbc = prs[dslot]
                t_r2 = self.vop(lambda e, dbc=dbc: e.reciprocal(dbc[64:65, 0:TT],
                                self.ps["pc"][64:65, 0:TT]),
                                waits=[t_ctx_mm] + ([pr_dep[dslot]] if dslot in pr_dep else []))
                t_r3 = self.P.dma("sync", drow0[0:1, 0:TT],
                                  dbc[64:65, 0:TT],
                                  waits=[t_r2, getattr(self, "_t_dr", None)], q="qsb")
                self._t_dr = t_r3
                t_b = self.mm(self.ps["ub"][0:64, 0:TT], self.sb["ones_row"][0:1, 0:64],
                              drow0[0:1, 0:TT], start=True, stop=True,
                              waits=[t_r3] + self.bw("ub"))
                t_be = self.vop(lambda e, dbc=dbc: e.tensor_copy(dbc[0:64, 0:TT],
                                self.ps["ub"][0:64, 0:TT]), waits=[t_b])
                self.bs("ub", t_be)
                if h % 2 == 0:
                    dstc = ctx1[0:64, (h // 2) * TT:(h // 2 + 1) * TT]
                    t_cev = self.vop(lambda e, d=dstc, dbc=dbc: e.tensor_mul(
                        d, self.ps["pc"][0:64, 0:TT], dbc[0:64, 0:TT]),
                        waits=[t_ctx_mm, t_be])
                else:
                    t_c0 = self.vop(lambda e, dbc=dbc: e.tensor_mul(stg[:, 0:TT],
                                    self.ps["pc"][0:64, 0:TT], dbc[0:64, 0:TT]),
                                    waits=[t_ctx_mm, t_be, getattr(self, "_t_stg", None)])
                    t_cev = self.P.dma("sync", ctx1[64:128, (h // 2) * TT:(h // 2 + 1) * TT],
                                       stg[:, 0:TT], waits=[t_c0], q="qsb")
                    self._t_stg = t_cev
                self.bs("pc", t_cev)
                pr_dep[dslot] = t_cev
            if upto <= 5:
                self.fence()
                return
            self.fence()
            t_ctx_all = [self.P.tok("V"), self.P.tok("qsb")]
            t_xqo = self.quant_chain(lambda cc: ctx1[:, cc * TT:(cc + 1) * TT],
                                     lambda cc: ctx1[:, cc * TT:(cc + 1) * TT],
                                     TT, C, waits=t_ctx_all)
            xqo = ctx1
            if upto <= 6:
                self.fence()
                return
            for oc in range(C):
                slot = oc % len(wtb)
                dep = self.slotdep.get("l1w" + str(slot))
                t_w = self.P.dma("sync", wtb[slot][:, 0:C * 128], self.l1wo_s[oc],
                                 waits=([dep] if dep else []) + self.t_l1w,
                                 q=["swC", "swD"][slot])
                pb = ["p0", "p1"][oc % 2]
                psm = self.ps[pb]
                t_mm = None
                for hc in range(NH1 // 2):
                    t_mm = self.mm(psm[:, 0:TT],
                                   wtb[slot][:, hc * 128:(hc + 1) * 128],
                                   xqo[:, hc * TT:(hc + 1) * TT],
                                   start=(hc == 0), stop=(hc == NH1 // 2 - 1),
                                   waits=[t_w] + t_xqo + (self.bw(pb) if hc == 0 else []),
                                   force_inc=True)
                self.slotdep["l1w" + str(slot)] = t_mm
                dst = X[:, oc * S + g * TT: oc * S + (g + 1) * TT]
                t_ev = self.vop(lambda e, d=dst, p=psm: e.scalar_tensor_tensor(
                    d, p[:, 0:TT], 0.5, d, AluOpType.mult, AluOpType.add), waits=[t_mm])
                self.bs(pb, t_ev)
            self.fence()
        self.free("k1", "v1t", "drow0", "stg", "m1q_t", "xsf", "sgnq", "qint", "scl_b",
                  "rscl_b", "q1g")
        self.t_l1a = self.P.tok("V")

    def l1_ffn_weights(self):
        c = self.cfg
        C, CF1 = c["C"], c["CF1"]
        gw = self.din("gate_prep", [CF1, 128, C * 128], I32)
        uw = self.din("up_prep", [CF1, 128, C * 128], I32)
        dw = self.din("down_prep", [C, 128, CF1 * 128], I32)
        self.g_s = self.dscr("g_s", [CF1, 128, C * 128], BF16)
        self.u_s = self.dscr("u_s", [CF1, 128, C * 128], BF16)
        self.d_s = self.dscr("d_s", [C, 128, CF1 * 128], BF16)
        CW = min(1024, C * 128)
        ib = [self.sbuf("fwi0", [128, CW], I32), self.sbuf("fwi1", [128, CW], I32)]
        bb = [self.sbuf("fwb0", [128, CW], BF16), self.sbuf("fwb1", [128, CW], BF16)]
        dep_load = {}
        prev_store = {}
        trips = ((gw, self.g_s), (uw, self.u_s), (dw, self.d_s))
        sel = getattr(self, "_dbg_fw", None)
        if sel is not None:
            trips = trips[sel:sel + 1]
        i = 0
        for src_, dst in trips:
            for oc in range(src_.shape[0]):
                w = src_.shape[2]
                for a0 in range(0, w, CW):
                    n = min(CW, w - a0)
                    s = i % 2; i += 1
                    t_w = self.P.dma("gpsimd", ib[s][:, 0:n], src_[oc][:, a0:a0 + n],
                                     waits=[dep_load.get(s)], q=["gwA", "gwB"][s])
                    t_c = self.vop(lambda e, s=s, n=n: e.tensor_copy(
                        bb[s][:, 0:n], ib[s][:, 0:n]),
                        waits=[t_w, prev_store.get(s)])
                    dep_load[s] = t_c
                    t_st = self.P.dma("sync", dst[oc][:, a0:a0 + n], bb[s][:, 0:n],
                                      waits=[t_c], q=["qscw", "qscx"][s])
                    prev_store[s] = t_st
        self.t_l1f_scr = [self.P.tok("qscw"), self.P.tok("qscx")]
        self.fence()
        self.free("fwi0", "fwi1", "fwb0", "fwb1")

    def l1_ffn(self):
        c = self.cfg
        S, TT, C, NT, CF1 = c["S"], c["TT"], c["C"], c["NT"], c["CF1"]
        X = self.sb["X"]
        NQ = CF1 // 4
        lnx = self.RGA
        h_q = self.sbuf("h_q1", [128, NQ * TT], BF16)
        upb0 = self.sbuf("upb0", [128, C * 128], BF16)
        upb1 = self.sbuf("upb1", [128, C * 128], BF16)
        dnb0 = self.sbuf("dnb0", [128, NQ * 128], BF16)
        dnb1 = self.sbuf("dnb1", [128, NQ * 128], BF16)
        gsc = self.din("gate_scale_cols", [128, CF1], F32)
        usc = self.din("up_scale_cols", [128, CF1], F32)
        dsc = self.din("down_scale_cols", [128, C], F32)
        gss = self.sbuf("gss", [128, CF1], F32, small=(CF1 * 4 <= 1024))
        uss = self.sbuf("uss", [128, CF1], F32, small=(CF1 * 4 <= 1024))
        dss = self.sbuf("dss", [128, C], F32, small=True)
        self.P.dma("sync", gss, gsc, q="qx")
        self.P.dma("sync", uss, usc, q="qx")
        self.P.dma("sync", dss, dsc, q="qx")
        t_sc = self.P.tok("qx")
        gub = [self.sb["gw0"], self.sbuf("gab1", [128, C * 128], BF16)]
        upb = [upb0, upb1]
        dnb = [dnb0, dnb1]
        sgn = self.sbuf("sgn_f", [128, 512], BF16)
        for ti in range(NT):
            st = self.stats_plain(
                lambda cc, ti=ti: X[:, cc * S + ti * TT: cc * S + (ti + 1) * TT],
                TT, C, waits=[self.P.tok("T"), self.P.tok("V")])
            t_ln = []
            for cc in range(C):
                t_ln.append(self.ln_apply2(lnx[:, cc * TT:(cc + 1) * TT],
                                           X[:, cc * S + ti * TT: cc * S + (ti + 1) * TT],
                                           waits=[st["nmu_b"], st["r_b"], self.P.tok("T")]))
            for cc in range(C):
                self.mm(self.ps["u0"][0:1, 0:TT], self.sb["ones_col"],
                        lnx[:, cc * TT:(cc + 1) * TT], start=(cc == 0), stop=(cc == C - 1),
                        waits=[t_ln[cc]] + (self.bw("u0") if cc == 0 else []))
            t_bs = self.P.tok("T")
            t_brow = self.vop(lambda e: e.tensor_scalar_mul(self.sb["row_bf"][0:1, 0:TT],
                              self.ps["u0"][0:1, 0:TT], -128.0), waits=[t_bs])
            self.bs("u0", t_brow)
            for fq in range(4):
                t_h = []
                for j in range(NQ):
                    fc = fq * NQ + j
                    slot = fc % 2
                    dep = self.slotdep.get("gu" + str(slot))
                    t_wg = self.P.dma("sync", gub[slot][:, 0:C * 128], self.g_s[fc],
                                      waits=([dep] if dep else []) + self.t_l1f_scr,
                                      q=["swA", "swB"][slot])
                    psm = self.ps["p0"]
                    t_mm = None
                    for dc in range(C):
                        t_mm = self.mm(psm[:, 0:TT], gub[slot][:, dc * 128:(dc + 1) * 128],
                                       lnx[:, dc * TT:(dc + 1) * TT], start=(dc == 0),
                                       stop=False,
                                       waits=[t_wg] + t_ln + (self.bw("p0") if dc == 0 else []))
                    t_mm = self.mm(psm[:, 0:TT], self.sb["ones_row"],
                                   self.sb["row_bf"][0:1, 0:TT], start=False, stop=True,
                                   waits=[t_brow])
                    t_g = self.sop(lambda e, p=psm, fc=fc: e.activation(
                        sgn[:, 0:TT], p[:, 0:TT], AF.Silu, scale=gss[:, fc:fc + 1]),
                        waits=[t_mm, t_sc] + t_h[-1:])
                    self.bs("p0", t_g)
                    self.slotdep["gu" + str(slot)] = t_g
                    dep = self.slotdep.get("uu" + str(slot))
                    t_wu = self.P.dma("sync", upb[slot][:, 0:C * 128], self.u_s[fc],
                                      waits=([dep] if dep else []) + self.t_l1f_scr,
                                      q=["swC", "swD"][slot])
                    psm2 = self.ps["p1"]
                    t_mm2 = None
                    for dc in range(C):
                        t_mm2 = self.mm(psm2[:, 0:TT], upb[slot][:, dc * 128:(dc + 1) * 128],
                                        lnx[:, dc * TT:(dc + 1) * TT], start=(dc == 0),
                                        stop=False,
                                        waits=[t_wu] + t_ln + (self.bw("p1") if dc == 0 else []))
                    t_mm2 = self.mm(psm2[:, 0:TT], self.sb["ones_row"],
                                    self.sb["row_bf"][0:1, 0:TT], start=False, stop=True,
                                    waits=[t_brow])
                    self.slotdep["uu" + str(slot)] = t_mm2
                    t_hh = self.vop(lambda e, p=psm2, fc=fc, j=j: e.scalar_tensor_tensor(
                        h_q[:, j * TT:(j + 1) * TT], p[:, 0:TT], uss[:, fc:fc + 1],
                        sgn[:, 0:TT], AluOpType.mult, AluOpType.mult),
                        waits=[t_mm2, t_g, t_sc])
                    self.bs("p1", t_hh)
                    t_h.append(t_hh)
                for j in range(NQ):
                    self.mm(self.ps["u0"][0:1, 0:TT], self.sb["ones_col"],
                            h_q[:, j * TT:(j + 1) * TT], start=(j == 0), stop=(j == NQ - 1),
                            waits=[t_h[j]] + (self.bw("u0") if j == 0 else []))
                t_bh = self.P.tok("T")
                t_bh1 = self.vop(lambda e: e.tensor_scalar_mul(self.sb["row_bf2"][0:1, 0:TT],
                                 self.ps["u0"][0:1, 0:TT], -128.0), waits=[t_bh])
                self.bs("u0", t_bh1)
                for oc in range(C):
                    slot = oc % 2
                    dep = self.slotdep.get("dn" + str(slot))
                    t_wd = self.P.dma("gpsimd", dnb[slot][:, 0:NQ * 128],
                                      self.d_s[oc][:, fq * NQ * 128:(fq + 1) * NQ * 128],
                                      waits=([dep] if dep else []) + self.t_l1f_scr,
                                      q=["gwA", "gwB"][slot])
                    pb = ["p0", "p1"][oc % 2]
                    psm = self.ps[pb]
                    t_mm = None
                    for j in range(NQ):
                        t_mm = self.mm(psm[:, 0:TT], dnb[slot][:, j * 128:(j + 1) * 128],
                                       h_q[:, j * TT:(j + 1) * TT], start=(j == 0),
                                       stop=False,
                                       waits=[t_wd] + t_h + (self.bw(pb) if j == 0 else []))
                    t_mm = self.mm(psm[:, 0:TT], self.sb["ones_row"],
                                   self.sb["row_bf2"][0:1, 0:TT], start=False, stop=True,
                                   waits=[t_bh1])
                    self.slotdep["dn" + str(slot)] = t_mm
                    dst = X[:, oc * S + ti * TT: oc * S + (ti + 1) * TT]
                    t_ev = self.vop(lambda e, d=dst, p=psm, oc=oc: e.scalar_tensor_tensor(
                        d, p[:, 0:TT], dss[:, oc:oc + 1], d, AluOpType.mult, AluOpType.add),
                        waits=[t_mm, t_sc])
                    self.bs(pb, t_ev)
        self.t_l1 = self.P.tok("V")

    def store_x(self):
        c = self.cfg
        xout = self.dout("x_out", [128, c["C"] * c["S"]], F32)
        n = c["C"] * c["S"]
        t_all = [self.P.tok("V"), self.P.tok("T"), self.P.tok("S")]
        q = n // 4
        for i in range(4):
            self.P.dma("sync", xout[:, i * q:(i + 1) * q], self.sb["X"][:, i * q:(i + 1) * q],
                       waits=t_all, q="qst")
        self.P.op("sync", lambda e: e.nop(), waits=[self.P.tok("qst")])

    def debug_out(self, name, ap, waits):
        o = self.dout("dbg_" + name, list(ap.shape), F32)
        sbf = self.sbuf("dbgs_" + name, list(ap.shape), F32)
        t = self.vop(lambda e: e.tensor_copy(sbf, ap), waits=waits)
        return self.P.dma("sync", o, sbf, waits=[t], q="qst")

    def build_all(self):
        self.setup()
        self.load_x()
        self.l0_wffn_prep_early()
        self.l0_phaseA()
        self.l0_vsuf()
        self.l0_phaseB()
        self.fence()
        self.free("k0", "v0t")
        for b in range(self.cfg["NT"] - 1):
            if f"vsuf{b}" in self._live:
                self.free(f"vsuf{b}")
        self.l0_wffn_prep2()
        self.l0_ffn()
        self.l1_weights()
        self.l1_ffn_weights()
        self.l1_main()
        self.fence()
        self.l1_ffn()
        self.store_x()
        self.arena_check()
        self.P.build()
        return self.nc

    def l0_wffn_prep_early(self):
        self.l0_wffn_prep()

    def finish(self):
        self.arena_check()
        self.P.build()
        return self.nc


def prep_stream(W):
    din, dout = W.shape
    a = W.reshape(din // 128, 128, dout // 128, 128).transpose(2, 1, 0, 3)
    return np.ascontiguousarray(a).reshape(dout // 128, 128, din // 128 * 128)

def prep_x(xb):
    S, D = xb.shape
    C = D // 128
    return np.ascontiguousarray(xb.T.reshape(C, 128, S).transpose(1, 0, 2).reshape(128, C * S))

def unprep_x(a, S, D):
    C = D // 128
    return a.reshape(128, C, S).transpose(2, 1, 0).reshape(S, D)

def cols(v):
    return np.ascontiguousarray(v.reshape(-1, 128).T)

def l1_q_perm(NH1, KV1):
    rep1 = NH1 // KV1
    A = [h for h in range(NH1) if ((h // rep1) % 2) == 0]
    B = [h for h in range(NH1) if ((h // rep1) % 2) == 1]
    perm = []
    for j in range(NH1 // 2):
        perm.extend(range(A[j] * 64, A[j] * 64 + 64))
        perm.extend(range(B[j] * 64, B[j] * 64 + 64))
    return np.array(perm)


_BUILD_CACHE = {}

def _build():
    if "nc" not in _BUILD_CACHE:
        m = MK(real_cfg())
        nc = m.build_all()
        _BUILD_CACHE["nc"] = nc
    return _BUILD_CACHE["nc"]


def kernel(L0_attn_hidden_states, L0_attn_Wq, L0_attn_Wk, L0_attn_Wv, L0_attn_Wo,
           L0_ffn_w1, L0_ffn_b1, L0_ffn_w2, L0_ffn_b2,
           L1_attn_wq, L1_attn_wk, L1_attn_wv, L1_attn_wo, L1_attn_rms_w,
           L1_ffn_gate_w, L1_ffn_gate_scale, L1_ffn_up_w, L1_ffn_up_scale,
           L1_ffn_down_w, L1_ffn_down_scale):
    import numpy as np
    from concourse.bass_utils import run_bass_kernel_spmd
    f32 = np.float32
    x = np.asarray(L0_attn_hidden_states, dtype=f32)
    B, S, D = x.shape
    C = D // 128
    perm = l1_q_perm(32, 4)
    asc = np.ascontiguousarray
    common = {
        "wq_prep": prep_stream(np.asarray(L0_attn_Wq, f32)),
        "wk_prep": prep_stream(np.asarray(L0_attn_Wk, f32)),
        "wv_prep": asc(np.asarray(L0_attn_Wv, f32).reshape(C, 128, -1)),
        "wo_prep": prep_stream(np.asarray(L0_attn_Wo, f32)),
        "w1_prep": prep_stream(asc(np.asarray(L0_ffn_w1, f32).T)),
        "w2_prep": prep_stream(asc(np.asarray(L0_ffn_w2, f32).T)),
        "b1_cols": cols(np.asarray(L0_ffn_b1, f32)),
        "b2_row": asc(np.asarray(L0_ffn_b2, f32).reshape(1, -1)),
        "l1wq_prep": prep_stream(asc(np.asarray(L1_attn_wq, f32).T[:, perm])),
        "l1wk_prep": prep_stream(asc(np.asarray(L1_attn_wk, f32).T)),
        "l1wv_prep": asc(np.asarray(L1_attn_wv, f32).T.reshape(C, 128, -1)),
        "l1wo_prep": prep_stream(asc(np.asarray(L1_attn_wo, f32).T)),
        "rms_cols": cols(np.asarray(L1_attn_rms_w, f32)),
        "gate_prep": prep_stream(asc(np.asarray(L1_ffn_gate_w, np.int32).T)),
        "up_prep": prep_stream(asc(np.asarray(L1_ffn_up_w, np.int32).T)),
        "down_prep": prep_stream(asc(np.asarray(L1_ffn_down_w, np.int32).T)),
        "gate_scale_cols": cols(np.asarray(L1_ffn_gate_scale, f32)),
        "up_scale_cols": cols(np.asarray(L1_ffn_up_scale, f32)),
        "down_scale_cols": cols(np.asarray(L1_ffn_down_scale, f32)),
    }
    in_maps = []
    for b in range(B):
        d = dict(common)
        d["x_in"] = prep_x(x[b])
        in_maps.append(d)
    nc = _build()
    res = run_bass_kernel_spmd(nc, in_maps, core_ids=list(range(B)))
    outs = [unprep_x(res.results[b]["x_out"], S, D) for b in range(B)]
    return np.stack(outs, axis=0)
